# version 02
# baseline (speedup 1.0000x reference)
import numpy as np
from contextlib import ExitStack
import concourse.bass as bass
import concourse.mybir as mybir
from concourse.bass_isa import ReduceOp

F32 = mybir.dt.float32
F32R = mybir.dt.float32r
BF16 = mybir.dt.bfloat16
F16 = mybir.dt.float16
AF = mybir.ActivationFunctionType
ALU = mybir.AluOpType
AX = mybir.AxisListType

ENGINES = ("v", "s", "t", "g", "y")
NQ = 12

class Prog:

    def __init__(self, nc):
        self.nc = nc
        self.ops = {e: [] for e in ENGINES}
        self.count = {}
        self.sem_names = ["SM_" + e for e in ENGINES]
        self.qpool = {}
        for q in ("y", "s", "g"):
            self.qpool[q] = ["Q%s%d" % (q, i) for i in range(NQ)]
            self.sem_names += self.qpool[q]
        for s_ in self.sem_names:
            self.count[s_] = 0
        self.qidx = {q: 0 for q in self.qpool}
        self.eng_sem = {e: "SM_" + e for e in ENGINES}
        self.wm = {e: {} for e in ENGINES}
        self.sem_ctx = None

    def _wait(self, e, deps):
        own = self.eng_sem[e]
        for d in deps:
            if d is None:
                continue
            sem, val = d
            if val <= 0:
                continue
            if e == "t" and sem == own:
                continue
            cur = self.wm[e].get(sem, 0)
            if val > cur:
                self.wm[e][sem] = val
                self.ops[e].append(("w", sem, val))

    def op(self, e, emit, deps=()):
        self._wait(e, deps)
        sem = self.eng_sem[e]
        self.count[sem] += 1
        self.ops[e].append(("i", emit, sem))
        return (sem, self.count[sem])

    def dma(self, e, emit, deps=()):
        pool = self.qpool[e]
        sem = pool[self.qidx[e] % NQ]
        self.qidx[e] += 1
        prior = self.count[sem]
        alldeps = list(deps)
        if prior > 0:
            alldeps.append((sem, prior))
        self._wait(e, alldeps)
        self.count[sem] += 16
        self.ops[e].append(("i", emit, sem))
        return (sem, self.count[sem])

    def raw(self, e, emit, deps=()):
        self._wait(e, deps)
        self.ops[e].append(("r", emit))
        return None

    def barrier(self, e, deps):
        self._wait(e, deps)

    def build_blocks(self, block):
        nc = self.nc
        handles = self.sem_ctx
        engmap = {"v": nc.vector, "s": nc.scalar, "t": nc.tensor, "g": nc.gpsimd, "y": nc.sync}
        def mk(e):
            eng = engmap[e]
            def body(_):
                for item in self.ops[e]:
                    if item[0] == "w":
                        eng.wait_ge(handles[item[1]], item[2])
                    elif item[0] == "i":
                        inst = item[1](eng, nc)
                        inc = 16 if item[2] not in self.eng_sem.values() else 1
                        inst.then_inc(handles[item[2]], inc)
                    else:
                        item[1](eng, nc)
            return body
        for e in ENGINES:
            if self.ops[e]:
                getattr(block, {"v": "vector", "s": "scalar", "t": "tensor", "g": "gpsimd", "y": "sync"}[e])(mk(e))


class TBuf:
    __slots__ = ("wtok", "rtoks")
    def __init__(self):
        self.wtok = None
        self.rtoks = []
    def rdeps(self):
        return [self.wtok] if self.wtok else []
    def wdeps(self):
        d = list(self.rtoks)
        if self.wtok:
            d.append(self.wtok)
        return d
    def setw(self, tok):
        self.wtok = tok
        self.rtoks = []
    def addr(self, tok):
        if tok:
            self.rtoks.append(tok)


def finalize(nc):
    import bass_rust
    import concourse.mybir as _mb
    from concourse.library_config import all_libraries, standard
    mask = {}
    for lib in all_libraries:
        for it in lib.instructions:
            mask[it] = mask.get(it, 0) | (1 << lib.index)
    bass_rust.insert_library_loads(nc, mask, len(all_libraries), standard.index)
    _mb.codegen_inst_isa_subclasses(nc)
    return nc


D = 2048
DT = 16
R_LO = 64
NST = 16
E_XD = 96

def ceil_div(a, b): return (a + b - 1) // b

def build_L1(Lc):
    nc = bass.Bass("TRN2", target_bir_lowering=False, debug=False)
    xT = nc.dram_tensor("xT", [D, Lc], F32R, kind="ExternalInput").ap()
    wxpT = nc.dram_tensor("wxpT", [D, E_XD], F32R, kind="ExternalInput").ap()
    ones1 = nc.dram_tensor("ones1", [1, 128], F32, kind="ExternalInput").ap()
    onesc = nc.dram_tensor("onesc", [128, 1], F32R, kind="ExternalInput").ap()
    consts = nc.dram_tensor("consts", [128, 8], F32, kind="ExternalInput").ap()
    xn1T_o = nc.dram_tensor("xn1T", [D, Lc], F32, kind="ExternalOutput").ap()
    xdblT_o = nc.dram_tensor("xdblT", [E_XD, Lc], F32, kind="ExternalOutput").ap()

    CH = min(512, Lc)
    NCH = ceil_div(Lc, CH)

    with ExitStack() as st:
        E = st.enter_context
        XT = E(nc.sbuf_tensor("XT", [128, DT * Lc], F32R))
        XN = E(nc.sbuf_tensor("XN", [128, DT * Lc], F32R))
        WXP = E(nc.sbuf_tensor("WXP", [128, DT * E_XD], F32R))
        ONES = E(nc.sbuf_tensor("ONES", [1, 128], F32))
        ONESC = E(nc.sbuf_tensor("ONESC", [128, 1], F32R))
        CON = E(nc.sbuf_tensor("CON", [128, 8], F32))
        SQ = E(nc.sbuf_tensor("SQ", [128, Lc], F32R))
        ROWS = E(nc.sbuf_tensor("ROWS", [1, 8 * Lc], F32))
        XD = E(nc.sbuf_tensor("XD", [E_XD, Lc], F32))
        ps_sum = E(nc.psum_tensor("ps_sum", [1, Lc], F32))
        ps_sq = E(nc.psum_tensor("ps_sq", [1, Lc], F32))
        ps_bc1 = E(nc.psum_tensor("ps_bc1", [128, Lc], F32))
        ps_xd = E(nc.psum_tensor("ps_xd", [E_XD, Lc], F32))

        P = Prog(nc)
        P.sem_ctx = {name: E(nc.semaphore(name)) for name in P.sem_names}

        bXT, bXN, bSQ, bROWS, bXD = TBuf(), TBuf(), TBuf(), TBuf(), TBuf()
        bPS, bPB1, bPXD = TBuf(), TBuf(), TBuf()
        bONES, bWXP = TBuf(), TBuf()

        t = P.dma("y", lambda e, n, a=xT: e.dma_start(out=XT[:].rearrange("p (a t) -> p a t", a=DT), in_=a.rearrange("(a p) t -> p a t", p=128)))
        bXT.setw(t)
        t = P.dma("y", lambda e, n: e.dma_start(out=WXP[:].rearrange("p (a t) -> p a t", a=DT), in_=wxpT.rearrange("(a p) t -> p a t", p=128)))
        bWXP.setw(t)
        t = P.dma("y", lambda e, n: e.dma_start(out=ONES[:], in_=ones1[:]))
        bONES.setw(t)
        t = P.dma("y", lambda e, n: e.dma_start(out=ONESC[:], in_=onesc[:]))
        bONESC = TBuf(); bONESC.setw(t)
        tCON = P.dma("y", lambda e, n: e.dma_start(out=CON[:], in_=consts[:]))

        for a in range(DT):
            sl = slice(a * Lc, (a + 1) * Lc)
            tsq = P.op("s", lambda e, n, sl=sl: e.activation(out=SQ[:], in_=XT[:, sl], func=AF.Square),
                       deps=bXT.rdeps() + bSQ.wdeps())
            bSQ.setw(tsq)
            for c in range(NCH):
                cs = slice(c * CH, (c + 1) * CH)
                csl = slice(a * Lc + c * CH, a * Lc + (c + 1) * CH)
                tok = P.op("t", lambda e, n, csl=csl, cs=cs, a=a: e.matmul(
                    out=ps_sum[0:1, cs], lhsT=ONESC[:], rhs=XT[:, csl],
                    start=(a == 0), stop=(a == DT - 1)),
                    deps=bXT.rdeps() + bONESC.rdeps() + (bPS.wdeps() if a == 0 else []))
                bXT.addr(tok); bPS.setw(tok) if a == DT - 1 else None
                tok = P.op("t", lambda e, n, cs=cs, a=a: e.matmul(
                    out=ps_sq[0:1, cs], lhsT=ONESC[:], rhs=SQ[:, cs],
                    start=(a == 0), stop=(a == DT - 1)),
                    deps=bSQ.rdeps() + bONESC.rdeps())
                bSQ.addr(tok)
                if a == DT - 1:
                    bPS.setw(tok)

        mrow = ROWS[0:1, 0 * Lc:1 * Lc]
        sqm = ROWS[0:1, 1 * Lc:2 * Lc]
        var = ROWS[0:1, 2 * Lc:3 * Lc]
        srt = ROWS[0:1, 3 * Lc:4 * Lc]
        rstd = ROWS[0:1, 4 * Lc:5 * Lc]
        mr = ROWS[0:1, 5 * Lc:6 * Lc]
        m2 = ROWS[0:1, 6 * Lc:7 * Lc]
        t1 = P.op("v", lambda e, n: e.tensor_scalar(out=mrow, in0=ps_sum[0:1, :], scalar1=1.0 / D, scalar2=None, op0=ALU.mult), deps=bPS.rdeps())
        t2 = P.op("v", lambda e, n: e.tensor_scalar(out=sqm, in0=ps_sq[0:1, :], scalar1=1.0 / D, scalar2=None, op0=ALU.mult), deps=bPS.rdeps())
        bROWS.setw(t2)
        t3 = P.op("s", lambda e, n: e.activation(out=m2, in_=mrow, func=AF.Square), deps=[t1])
        t4 = P.op("v", lambda e, n: e.tensor_tensor(out=var, in0=sqm, in1=m2, op=ALU.subtract), deps=[t2, t3])
        t5 = P.op("s", lambda e, n: e.activation(out=srt, in_=var, func=AF.Sqrt, bias=CON[0:1, 0:1]), deps=[t4, tCON])
        t6 = P.op("v", lambda e, n: e.reciprocal(out=rstd, in_=srt), deps=[t5])
        t7 = P.op("v", lambda e, n: e.tensor_tensor(out=mr, in0=mrow, in1=rstd, op=ALU.mult), deps=[t6, t1])
        bROWS.setw(t7)

        for c in range(NCH):
            cs = slice(c * CH, (c + 1) * CH)
            tok = P.op("t", lambda e, n, cs=cs: e.matmul(out=ps_bc1[:, cs], lhsT=ONES[:],
                       rhs=rstd[0:1, cs], start=True, stop=True), deps=[t6] + bONES.rdeps() + bPB1.wdeps())
            bPB1.setw(tok)
        for a in range(DT):
            sl = slice(a * Lc, (a + 1) * Lc)
            tok = P.op("v", lambda e, n, sl=sl: e.tensor_tensor(out=XN[:, sl], in0=XT[:, sl], in1=ps_bc1[:, 0:Lc], op=ALU.mult),
                       deps=bPB1.rdeps() + bXT.rdeps() + bXN.wdeps())
            bXN.setw(tok); bPB1.addr(tok); bXT.addr(tok)
        for c in range(NCH):
            cs = slice(c * CH, (c + 1) * CH)
            tok = P.op("t", lambda e, n, cs=cs: e.matmul(out=ps_bc1[:, cs], lhsT=ONES[:],
                       rhs=mr[0:1, cs], start=True, stop=True), deps=[t7] + bONES.rdeps() + bPB1.wdeps())
            bPB1.setw(tok)
        for a in range(DT):
            sl = slice(a * Lc, (a + 1) * Lc)
            tok = P.op("v", lambda e, n, sl=sl: e.tensor_tensor(out=XT[:, sl], in0=XN[:, sl], in1=ps_bc1[:, 0:Lc], op=ALU.subtract),
                       deps=bPB1.rdeps() + bXN.rdeps() + bXT.wdeps())
            bXT.setw(tok); bPB1.addr(tok); bXN.addr(tok)

        for c in range(NCH):
            cs = slice(c * CH, (c + 1) * CH)
            for a in range(DT):
                csl = slice(a * Lc + c * CH, a * Lc + (c + 1) * CH)
                wsl = slice(a * E_XD, (a + 1) * E_XD)
                tok = P.op("t", lambda e, n, csl=csl, cs=cs, wsl=wsl, a=a: e.matmul(
                    out=ps_xd[:, cs], lhsT=WXP[:, wsl], rhs=XT[:, csl],
                    start=(a == 0), stop=(a == DT - 1)),
                    deps=bXT.rdeps() + bWXP.rdeps() + (bPXD.wdeps() if a == 0 else []))
                bXT.addr(tok)
                if a == DT - 1:
                    bPXD.setw(tok)
        tok = P.op("s", lambda e, n: e.activation(out=XD[:], in_=ps_xd[:], func=AF.Copy), deps=bPXD.rdeps())
        bXD.setw(tok); bPXD.addr(tok)

        tk = P.dma("g", lambda e, n: e.dma_start(out=xn1T_o.rearrange("(a p) t -> p a t", p=128), in_=XT[:].rearrange("p (a t) -> p a t", a=DT)), deps=bXT.rdeps())
        bXT.addr(tk)
        tk2 = P.dma("g", lambda e, n: e.dma_start(out=xdblT_o[:], in_=XD[:]), deps=bXD.rdeps())
        P.barrier("g", deps=[tk, tk2])
        block = E(nc.Block())
        P.build_blocks(block)
    return finalize(nc)

def build_L2(L, wcols):
    nc = bass.Bass("TRN2", target_bir_lowering=False, debug=False)
    xn1T = nc.dram_tensor("xn1T", [256, L], F32, kind="ExternalInput").ap()
    xdblT = nc.dram_tensor("xdblT", [E_XD, L], F32R, kind="ExternalInput").ap()
    wdtT = nc.dram_tensor("wdtT", [R_LO, 256], F32R, kind="ExternalInput").ap()
    bdt = nc.dram_tensor("bdt", [128, 2], F32, kind="ExternalInput").ap()
    alog = nc.dram_tensor("alog", [128, 32], F32, kind="ExternalInput").ap()
    dpar = nc.dram_tensor("dpar", [128, 2], F32, kind="ExternalInput").ap()
    ones1 = nc.dram_tensor("ones1", [1, 128], F32R, kind="ExternalInput").ap()
    ident = nc.dram_tensor("ident", [128, 128], F32R, kind="ExternalInput").ap()
    sel32 = nc.dram_tensor("sel32", [32, 32 * 128], F32R, kind="ExternalInput").ap()
    consts = nc.dram_tensor("consts", [128, 8], F32, kind="ExternalInput").ap()
    w1a = nc.dram_tensor("w1a", [128, wcols], F32, kind="ExternalInput").ap()
    w3a = nc.dram_tensor("w3a", [128, wcols], F32, kind="ExternalInput").ap()
    w2a = nc.dram_tensor("w2a", [128, wcols], F32, kind="ExternalInput").ap()
    yT_o = nc.dram_tensor("yT", [256, L], F32, kind="ExternalOutput").ap()
    wsum_o = nc.dram_tensor("wsum", [128, 6], F32, kind="ExternalOutput").ap()

    TC = min(1024, L)
    NCHK = L // TC
    NC2 = TC // 512 if TC >= 512 else 1
    C2 = min(512, TC)

    with ExitStack() as st:
        E = st.enter_context
        XN1 = E(nc.sbuf_tensor("XN1", [128, 2, 2 * TC], F32))
        XDR = E(nc.sbuf_tensor("XDR", [R_LO, L], F32R))
        XBC = E(nc.sbuf_tensor("XBC", [32, L], F32R))
        SEL = E(nc.sbuf_tensor("SEL", [32, 32 * 128], F32R))
        WDT = E(nc.sbuf_tensor("WDT", [R_LO, 256], F32R))
        BDT = E(nc.sbuf_tensor("BDT", [128, 2], F32))
        ALOG = E(nc.sbuf_tensor("ALOG", [128, 32], F32))
        AT = E(nc.sbuf_tensor("AT", [128, 32], F32))
        DP = E(nc.sbuf_tensor("DP", [128, 2], F32))
        ONES = E(nc.sbuf_tensor("ONES", [1, 128], F32R))
        IDENT = E(nc.sbuf_tensor("IDENT", [128, 128], F32R))
        DELTA = E(nc.sbuf_tensor("DELTA", [128, 2 * TC], F32))
        UU = E(nc.sbuf_tensor("UU", [128, 2 * TC], F32))
        SA = E(nc.sbuf_tensor("SA", [128, TC], F32))
        SB = E(nc.sbuf_tensor("SB", [128, TC], F32))
        SC = E(nc.sbuf_tensor("SC", [128, TC], F32))
        SD = E(nc.sbuf_tensor("SD", [128, TC], F32))
        DA = E(nc.sbuf_tensor("DA", [128, TC], F32))
        DBX = E(nc.sbuf_tensor("DBX", [128, TC], F32))
        HH = E(nc.sbuf_tensor("HH", [128, TC], F32))
        ZZ = E(nc.sbuf_tensor("ZZ", [128, TC], F32R))
        ST = E(nc.sbuf_tensor("STT", [128, 32], F32))
        YO = E(nc.sbuf_tensor("YO", [128, 2 * TC], F32))
        WABS = E(nc.sbuf_tensor("WABS", [128, wcols // 2], F32))
        WS = E(nc.sbuf_tensor("WS", [128, 8], F32))
        CON = E(nc.sbuf_tensor("CON", [128, 8], F32))
        ps_delta = E(nc.psum_tensor("ps_delta", [128, TC], F32))
        ps_B = E(nc.psum_tensor("ps_B", [128, TC], F32))
        ps_C = E(nc.psum_tensor("ps_C", [128, TC], F32))
        ps_y = E(nc.psum_tensor("ps_y", [128, TC], F32))

        P = Prog(nc)
        P.sem_ctx = {name: E(nc.semaphore(name)) for name in P.sem_names}

        bXN1, bXDB, bWDT, bBDT, bALOG, bAT, bDP, bONES, bIDENT = (TBuf() for _ in range(9))
        bDELTA, bUU, bSA, bSB, bSC, bSD, bDA, bDBX, bHH, bZZ, bST, bYO = (TBuf() for _ in range(12))
        bWABS, bWS = TBuf(), TBuf()
        bPD, bPB, bPC, bPY = TBuf(), TBuf(), TBuf(), TBuf()

        bXN1b = [TBuf(), TBuf()]
        bXDB.setw(P.dma("y", lambda e, n: e.dma_start(out=XDR[:], in_=xdblT[0:R_LO, :])))
        bXBC = TBuf()
        bXBC.setw(P.dma("y", lambda e, n: e.dma_start(out=XBC[:], in_=xdblT[R_LO:E_XD, :])))
        bSEL = TBuf()
        bSEL.setw(P.dma("y", lambda e, n: e.dma_start(out=SEL[:], in_=sel32[:])))
        bWDT.setw(P.dma("y", lambda e, n: e.dma_start(out=WDT[:], in_=wdtT[:])))
        bBDT.setw(P.dma("y", lambda e, n: e.dma_start(out=BDT[:], in_=bdt[:])))
        bALOG.setw(P.dma("y", lambda e, n: e.dma_start(out=ALOG[:], in_=alog[:])))
        bDP.setw(P.dma("y", lambda e, n: e.dma_start(out=DP[:], in_=dpar[:])))
        bONES.setw(P.dma("y", lambda e, n: e.dma_start(out=ONES[:], in_=ones1[:])))
        bIDENT.setw(P.dma("y", lambda e, n: e.dma_start(out=IDENT[:], in_=ident[:])))
        tCON = P.dma("y", lambda e, n: e.dma_start(out=CON[:], in_=consts[:]))

        t1 = P.op("s", lambda e, n: e.activation(out=AT[:], in_=ALOG[:], func=AF.Exp), deps=bALOG.rdeps())
        t2 = P.op("v", lambda e, n: e.tensor_scalar(out=AT[:], in0=AT[:], scalar1=-1.0, scalar2=None, op0=ALU.mult), deps=[t1])
        bAT.setw(t2)

        bST.setw(P.op("v", lambda e, n: e.memset(ST[:], 0.0)))

        wc2 = wcols // 2
        for m, win in enumerate((w1a, w3a, w2a)):
            for piece in range(2):
                tk = P.dma("y", lambda e, n, win=win, piece=piece: e.dma_start(
                    out=WABS[:], in_=win[:, piece * wc2:(piece + 1) * wc2]), deps=bWABS.wdeps())
                bWABS.setw(tk)
                col = 2 * m + piece
                ts_ = P.op("s", lambda e, n, col=col: e.activation(out=WABS[:], in_=WABS[:], func=AF.Abs,
                           accum_out=WS[:, col:col + 1]), deps=bWABS.rdeps() + bWS.wdeps())
                bWABS.setw(ts_); bWS.setw(ts_)
        store_toks = []
        tko = P.dma("g", lambda e, n: e.dma_start(out=wsum_o[:], in_=WS[:, 0:6]), deps=bWS.rdeps())
        bWS.addr(tko); store_toks.append(tko)

        for c in range(NCHK):
            tsl = slice(c * TC, (c + 1) * TC)
            cb = c % 2
            bXN1 = bXN1b[cb]
            tok = P.dma("y", lambda e, n, c=c, cb=cb: e.dma_start(
                out=XN1[:, cb].rearrange("p (h t) -> p h t", h=2),
                in_=xn1T.rearrange("(h p) t -> p h t", p=128)[:, :, c * TC:(c + 1) * TC]),
                deps=bXN1.wdeps())
            bXN1.setw(tok)
            for half in range(2):
                hsl = slice(half * TC, (half + 1) * TC)
                xsl = slice(half * TC, (half + 1) * TC)
                for c2 in range(NC2):
                    c2o = slice(c2 * C2, (c2 + 1) * C2)
                    rsl = slice(c * TC + c2 * C2, c * TC + (c2 + 1) * C2)
                    tok = P.op("t", lambda e, n, c2o=c2o, rsl=rsl, half=half: e.matmul(
                        out=ps_delta[:, c2o], lhsT=WDT[:, half * 128:(half + 1) * 128],
                        rhs=XDR[:, rsl], start=True, stop=True),
                        deps=bXDB.rdeps() + bWDT.rdeps() + bPD.wdeps())
                    bPD.setw(tok)
                tok = P.op("s", lambda e, n, half=half: e.activation(out=SA[:], in_=ps_delta[:], func=AF.Abs,
                           bias=BDT[:, half:half + 1]), deps=bPD.rdeps() + bBDT.rdeps() + bSA.wdeps())
                bSA.setw(tok); bPD.addr(tok)
                tok = P.op("s", lambda e, n: e.activation(out=SB[:], in_=SA[:], func=AF.Exp, scale=-1.0),
                           deps=bSA.rdeps() + bSB.wdeps())
                bSB.setw(tok); bSA.addr(tok)
                tok = P.op("s", lambda e, n: e.activation(out=SC[:], in_=SB[:], func=AF.Ln, bias=CON[0:128, 1:2]),
                           deps=bSB.rdeps() + bSC.wdeps() + [tCON])
                bSC.setw(tok); bSB.addr(tok)
                tok = P.op("s", lambda e, n, half=half: e.activation(out=SD[:], in_=ps_delta[:], func=AF.Relu,
                           bias=BDT[:, half:half + 1]), deps=bPD.rdeps() + bBDT.rdeps() + bSD.wdeps())
                bSD.setw(tok); bPD.addr(tok)
                tok = P.op("v", lambda e, n, hsl=hsl: e.tensor_tensor(out=DELTA[:, hsl], in0=SC[:], in1=SD[:], op=ALU.add),
                           deps=bSC.rdeps() + bSD.rdeps() + bDELTA.wdeps())
                bDELTA.setw(tok); bSC.addr(tok); bSD.addr(tok)
                tok = P.op("v", lambda e, n, hsl=hsl, xsl=xsl, cb=cb: e.tensor_tensor(
                    out=UU[:, hsl], in0=DELTA[:, hsl], in1=XN1[:, cb, xsl], op=ALU.mult),
                    deps=bDELTA.rdeps() + bXN1.rdeps() + bUU.wdeps())
                bUU.setw(tok); bDELTA.addr(tok); bXN1.addr(tok)

            for half in range(2):
                hsl = slice(half * TC, (half + 1) * TC)
                xsl = slice(half * TC, (half + 1) * TC)
                for nn in range(NST):
                    col = nn * 2 + half
                    for c2 in range(NC2):
                        c2o = slice(c2 * C2, (c2 + 1) * C2)
                        rsl = slice(c * TC + c2 * C2, c * TC + (c2 + 1) * C2)
                        tok = P.op("t", lambda e, n, c2o=c2o, rsl=rsl, nn=nn: e.matmul(
                            out=ps_B[:, c2o], lhsT=SEL[:, nn * 128:(nn + 1) * 128], rhs=XBC[:, rsl],
                            start=True, stop=True), deps=bXBC.rdeps() + bSEL.rdeps() + bPB.wdeps())
                        bPB.setw(tok)
                        tok = P.op("t", lambda e, n, c2o=c2o, rsl=rsl, nn=nn: e.matmul(
                            out=ps_C[:, c2o], lhsT=SEL[:, (NST + nn) * 128:(NST + nn + 1) * 128], rhs=XBC[:, rsl],
                            start=True, stop=True), deps=bXBC.rdeps() + bSEL.rdeps() + bPC.wdeps())
                        bPC.setw(tok)
                    tok = P.op("s", lambda e, n, hsl=hsl, col=col: e.activation(
                        out=DA[:], in_=DELTA[:, hsl], func=AF.Exp, scale=AT[:, col:col + 1]),
                        deps=bDELTA.rdeps() + bAT.rdeps() + bDA.wdeps())
                    bDA.setw(tok); bDELTA.addr(tok)
                    tok = P.op("v", lambda e, n, hsl=hsl: e.tensor_tensor(
                        out=DBX[:], in0=UU[:, hsl], in1=ps_B[:], op=ALU.mult),
                        deps=bUU.rdeps() + bPB.rdeps() + bDBX.wdeps())
                    bDBX.setw(tok); bUU.addr(tok); bPB.addr(tok)
                    tok = P.op("v", lambda e, n, col=col: e.tensor_tensor_scan(
                        out=HH[:], data0=DA[:], data1=DBX[:], initial=ST[:, col:col + 1],
                        op0=ALU.mult, op1=ALU.add),
                        deps=bDA.rdeps() + bDBX.rdeps() + bST.rdeps() + bHH.wdeps())
                    bHH.setw(tok); bDA.addr(tok); bDBX.addr(tok); bST.addr(tok)
                    tok = P.op("v", lambda e, n, col=col: e.tensor_copy(out=ST[:, col:col + 1], in_=HH[:, TC - 1:TC]),
                               deps=bHH.rdeps() + bST.wdeps())
                    bST.setw(tok); bHH.addr(tok)
                    tok = P.op("v", lambda e, n: e.tensor_tensor(out=ZZ[:], in0=HH[:], in1=ps_C[:], op=ALU.mult),
                               deps=bHH.rdeps() + bPC.rdeps() + bZZ.wdeps())
                    bZZ.setw(tok); bHH.addr(tok); bPC.addr(tok)
                    for c2 in range(NC2):
                        c2o = slice(c2 * C2, (c2 + 1) * C2)
                        tok = P.op("t", lambda e, n, c2o=c2o, nn=nn: e.matmul(
                            out=ps_y[:, c2o], lhsT=IDENT[:], rhs=ZZ[:, c2o],
                            start=(nn == 0), stop=(nn == NST - 1)),
                            deps=bZZ.rdeps() + bIDENT.rdeps() + (bPY.wdeps() if nn == 0 else []))
                        if nn == NST - 1:
                            bPY.setw(tok)
                        bZZ.addr(tok)
                tok = P.op("v", lambda e, n, hsl=hsl, xsl=xsl, half=half, cb=cb: e.scalar_tensor_tensor(
                    out=YO[:, hsl], in0=XN1[:, cb, xsl], scalar=DP[:, half:half + 1], in1=ps_y[:],
                    op0=ALU.mult, op1=ALU.add),
                    deps=bXN1.rdeps() + bDP.rdeps() + bPY.rdeps() + bYO.wdeps())
                bYO.setw(tok); bPY.addr(tok); bXN1.addr(tok)
            tok = P.dma("g", lambda e, n, c=c: e.dma_start(
                out=yT_o.rearrange("(h p) t -> p h t", p=128)[:, :, c * TC:(c + 1) * TC],
                in_=YO[:].rearrange("p (h t) -> p h t", h=2)), deps=bYO.rdeps())
            bYO.addr(tok); store_toks.append(tok)

        P.barrier("g", store_toks)
        block = E(nc.Block())
        P.build_blocks(block)
    return finalize(nc)

H = 5632
HT = 44

def build_L3(Lc, debug=False):
    nc = bass.Bass("TRN2", target_bir_lowering=False, debug=False)
    xT = nc.dram_tensor("xT", [D, Lc], F32, kind="ExternalInput").ap()
    yT = nc.dram_tensor("yT", [D, Lc], F32R, kind="ExternalInput").ap()
    w1T = nc.dram_tensor("w1T", [D, H], F32, kind="ExternalInput").ap()
    w3T = nc.dram_tensor("w3T", [D, H], F32, kind="ExternalInput").ap()
    w2T = nc.dram_tensor("w2T", [H, D], F32, kind="ExternalInput").ap()
    wqs = nc.dram_tensor("wqs", [128, 48], F32, kind="ExternalInput").ap()
    onesf = nc.dram_tensor("onesf", [1, 128], F32, kind="ExternalInput").ap()
    onescr = nc.dram_tensor("onescr", [128, 1], F32R, kind="ExternalInput").ap()
    onescf = nc.dram_tensor("onescf", [128, 1], F32, kind="ExternalInput").ap()
    consts = nc.dram_tensor("consts", [128, 8], F32, kind="ExternalInput").ap()
    out_o = nc.dram_tensor("outT", [D, Lc], F32, kind="ExternalOutput").ap()
    hfs = None

    NW = float(D) * H
    W = min(256, Lc)
    NP = Lc // W

    hfs = nc.dram_tensor("hfspill", [H, W], F32, kind="Internal").ap()
    if debug:
        dbg_x2 = nc.dram_tensor("dbg_x2", [D, Lc], F32, kind="ExternalOutput").ap()
        dbg_xq = nc.dram_tensor("dbg_xq", [D, Lc], F32, kind="ExternalOutput").ap()
        dbg_hf = nc.dram_tensor("dbg_hf", [H, Lc], F32, kind="ExternalOutput").ap()
        dbg_b2 = nc.dram_tensor("dbg_b2", [D, Lc], F32, kind="ExternalOutput").ap()
    with ExitStack() as st:
        E = st.enter_context
        PA = E(nc.sbuf_tensor("PA", [128, 16 * W], F32R))
        PB = E(nc.sbuf_tensor("PB", [128, 16 * W], F32))
        PC = E(nc.sbuf_tensor("PC", [128, 16 * W], F32R))
        PD = E(nc.sbuf_tensor("PD", [128, 16 * W], F32R))
        XQ = E(nc.sbuf_tensor("XQ", [128, 16 * W], BF16))
        QF = E(nc.sbuf_tensor("QF", [128, 16 * W], F16))
        UB = E(nc.sbuf_tensor("UB", [128, 11264], F32))
        SMALL = E(nc.sbuf_tensor("SMALL", [128, 14 * W], F32))
        WQT = E(nc.sbuf_tensor("WQT", [128, 2 * 2048], BF16))
        W2B = E(nc.sbuf_tensor("W2B", [128, 44 * 128], F32))
        W2Q = E(nc.sbuf_tensor("W2Q", [128, 44 * 128], BF16))
        BCP = E(nc.sbuf_tensor("BCP", [128, 6 * W], F32))
        SWB = E(nc.sbuf_tensor("SWB", [128, 16], F32))
        WQS = E(nc.sbuf_tensor("WQS", [128, 48], F32))
        ONESCF = E(nc.sbuf_tensor("ONESCF", [128, 1], F32))
        WROW = E(nc.sbuf_tensor("WROW", [1, 16], F32))
        ROWS = E(nc.sbuf_tensor("ROWS2", [1, 16 * W], F32))
        ONESF = E(nc.sbuf_tensor("ONESF", [1, 128], F32))
        ONESC2 = E(nc.sbuf_tensor("ONESC2", [128, 1], F32R))
        CON = E(nc.sbuf_tensor("CON", [128, 8], F32))
        MGF = E(nc.sbuf_tensor("MGF", [128, 2048], F16))
        ps_a = E(nc.psum_tensor("ps_a", [128, W], F32))
        ps_b = E(nc.psum_tensor("ps_b", [128, W], F32))
        ps_c = E(nc.psum_tensor("ps_c", [128, W], F32))
        ps_d = E(nc.psum_tensor("ps_d", [128, W], F32))
        ps_r = E(nc.psum_tensor("ps_r", [1, W], F32))
        ps_r2 = E(nc.psum_tensor("ps_r2", [1, W], F32))

        P = Prog(nc)
        P.sem_ctx = {name: E(nc.semaphore(name)) for name in P.sem_names}
        nc.const_aps.register(mybir.dt.float32, 1536.0, CON[0:128, 2:3])

        B = lambda: TBuf()
        bPA, bPB, bPC, bPD, bXQ, bQF, bUB, bSMALL, bWQT, bW2B, bW2Q = (B() for _ in range(11))
        bBCP, bSWB, bWQS, bWROW, bROWS, bONESF, bCON, bMGF = (B() for _ in range(8))
        bpa, bpb, bpc, bpd, bpr, bpr2 = (B() for _ in range(6))
        bHFS = B()
        store_toks = []

        bONESF.setw(P.dma("y", lambda e, n: e.dma_start(out=ONESF[:], in_=onesf[:])))
        bONESC2 = B()
        bONESC2.setw(P.dma("y", lambda e, n: e.dma_start(out=ONESC2[:], in_=onescr[:])))
        bONESCF = B()
        bONESCF.setw(P.dma("y", lambda e, n: e.dma_start(out=ONESCF[:], in_=onescf[:])))
        bCON.setw(P.dma("y", lambda e, n: e.dma_start(out=CON[:], in_=consts[:])))
        bWQS.setw(P.dma("y", lambda e, n: e.dma_start(out=WQS[:], in_=wqs[:])))

        tts = []
        bpr_init = []
        for m in range(3):
            tmm = P.op("t", lambda e, n, m=m: e.matmul(
                out=ps_r[0:1, 0:16], lhsT=ONESCF[:], rhs=WQS[:, m * 16:(m + 1) * 16],
                start=True, stop=True), deps=bWQS.rdeps() + bONESCF.rdeps() + bpr.wdeps())
            bpr.setw(tmm)
            trd = P.op("v", lambda e, n, m=m: e.reduce_sum(
                out=WROW[0:1, m:m + 1], in_=ps_r[0:1, 0:16], axis=AX.X),
                deps=bpr.rdeps())
            bpr.addr(trd)
            tts.append(trd)
        t_swm = P.op("v", lambda e, n: e.tensor_scalar(out=WROW[0:1, 0:3], in0=WROW[0:1, 0:3],
                     scalar1=1.0 / NW, scalar2=1e-5, op0=ALU.mult, op1=ALU.max), deps=tts)
        t_inv = P.op("v", lambda e, n: e.reciprocal(out=WROW[0:1, 3:6], in_=WROW[0:1, 0:3]), deps=[t_swm])
        t_ncw = P.op("v", lambda e, n: e.tensor_scalar(out=WROW[0:1, 6:9], in0=WROW[0:1, 0:3],
                     scalar1=-1.45, scalar2=None, op0=ALU.mult), deps=[t_swm])
        t_pcw = P.op("v", lambda e, n: e.tensor_scalar(out=WROW[0:1, 9:12], in0=WROW[0:1, 0:3],
                     scalar1=1.45, scalar2=None, op0=ALU.mult), deps=[t_swm])
        bWROW.setw(t_pcw)
        tok = P.op("t", lambda e, n: e.matmul(out=ps_a[:, 0:12], lhsT=ONESF[:],
                   rhs=WROW[0:1, 0:12], start=True, stop=True),
                   deps=[t_swm, t_inv, t_ncw, t_pcw] + bONESF.rdeps() + bpa.wdeps())
        bpa.setw(tok)
        tok = P.op("s", lambda e, n: e.activation(out=SWB[:, 0:12], in_=ps_a[:, 0:12], func=AF.Copy),
                   deps=bpa.rdeps() + bSWB.wdeps())
        bSWB.setw(tok); bpa.addr(tok)

        def swm_c(m): return SWB[:, m:m + 1]
        def swinv_c(m): return SWB[:, 3 + m:4 + m]
        def ncw_c(m): return SWB[:, 6 + m:7 + m]
        def pcw_c(m): return SWB[:, 9 + m:10 + m]

        def row(i):
            return ROWS[0:1, i * W:(i + 1) * W]

        def bc(i):
            return BCP[:, i * W:(i + 1) * W]

        def bcast(row_ap, slot, deps):
            tok = P.op("t", lambda e, n, row_ap=row_ap: e.matmul(
                out=ps_d[:, 0:W], lhsT=ONESF[:], rhs=row_ap, start=True, stop=True),
                deps=deps + bONESF.rdeps() + bpd.wdeps())
            bpd.setw(tok); bROWS.addr(tok)
            tok2 = P.op("s", lambda e, n, slot=slot: e.activation(out=bc(slot), in_=ps_d[:, 0:W], func=AF.Copy),
                        deps=bpd.rdeps() + bBCP.wdeps())
            bBCP.setw(tok2); bpd.addr(tok2)
            return tok2

        def stats(TEN, bTEN, r0, sq_off):
            toks = []
            for a in range(16):
                tok = P.op("t", lambda e, n, a=a: e.matmul(
                    out=ps_r[0:1, :], lhsT=ONESC2[:], rhs=TEN[:, a * W:(a + 1) * W],
                    start=(a == 0), stop=(a == 15)),
                    deps=bTEN.rdeps() + bONESC2.rdeps() + (bpr.wdeps() if a == 0 else []))
                bTEN.addr(tok)
            bpr.setw(tok); toks.append(tok)
            for a in range(16):
                tok = P.op("s", lambda e, n, a=a, sq_off=sq_off: e.activation(
                    out=PA[:, sq_off + a * W:sq_off + (a + 1) * W], in_=TEN[:, a * W:(a + 1) * W], func=AF.Square),
                    deps=bTEN.rdeps() + bPA.wdeps())
                bPA.setw(tok); bTEN.addr(tok)
                tok2 = P.op("t", lambda e, n, a=a, sq_off=sq_off: e.matmul(
                    out=ps_r2[0:1, :], lhsT=ONESC2[:],
                    rhs=PA[:, sq_off + a * W:sq_off + (a + 1) * W],
                    start=(a == 0), stop=(a == 15)),
                    deps=bPA.rdeps() + bONESC2.rdeps() + (bpr2.wdeps() if a == 0 else []))
                bPA.addr(tok2)
            bpr2.setw(tok2)
            tm = P.op("v", lambda e, n, r0=r0: e.tensor_scalar(out=row(r0), in0=ps_r[0:1, :],
                      scalar1=1.0 / D, scalar2=None, op0=ALU.mult), deps=bpr.rdeps() + bROWS.wdeps())
            bpr.addr(tm); bROWS.setw(tm)
            tsq = P.op("v", lambda e, n, r0=r0: e.tensor_scalar(out=row(r0 + 3), in0=ps_r2[0:1, :],
                       scalar1=1.0 / D, scalar2=None, op0=ALU.mult), deps=bpr2.rdeps() + bROWS.wdeps())
            bpr2.addr(tsq); bROWS.setw(tsq)
            tm2 = P.op("s", lambda e, n, r0=r0: e.activation(out=row(r0 + 4), in_=row(r0), func=AF.Square), deps=[tm] + bROWS.wdeps())
            bROWS.setw(tm2)
            tv = P.op("v", lambda e, n, r0=r0: e.tensor_tensor(out=row(r0 + 3), in0=row(r0 + 3), in1=row(r0 + 4),
                      op=ALU.subtract), deps=[tsq, tm2] + bROWS.wdeps())
            bROWS.setw(tv)
            tsr = P.op("s", lambda e, n, r0=r0: e.activation(out=row(r0 + 4), in_=row(r0 + 3), func=AF.Sqrt,
                       bias=CON[0:1, 0:1]), deps=[tv] + bCON.rdeps() + bROWS.wdeps())
            bROWS.setw(tsr)
            trs = P.op("v", lambda e, n, r0=r0: e.reciprocal(out=row(r0 + 1), in_=row(r0 + 4)), deps=[tsr] + bROWS.wdeps())
            bROWS.setw(trs)
            tmr = P.op("v", lambda e, n, r0=r0: e.tensor_tensor(out=row(r0 + 2), in0=row(r0), in1=row(r0 + 1),
                       op=ALU.mult), deps=[trs, tm] + bROWS.wdeps())
            bROWS.setw(tmr)
            return trs, tmr

        hf_wtoks, hf_rtoks = [], []
        for hp in range(NP):
            csl = slice(hp * W, (hp + 1) * W)
            tYT = P.dma("y", lambda e, n, csl=csl: e.dma_start(
                out=PA[:].rearrange("p (a t) -> p a t", a=16),
                in_=yT.rearrange("(a p) t -> p a t", p=128)[:, :, csl]), deps=bPA.wdeps())
            bPA.setw(tYT)
            tXT = P.dma("y", lambda e, n, csl=csl: e.dma_start(
                out=PB[:].rearrange("p (a t) -> p a t", a=16),
                in_=xT.rearrange("(a p) t -> p a t", p=128)[:, :, csl]), deps=bPB.wdeps())
            bPB.setw(tXT)
            toks = []
            for a in range(16):
                tok = P.op("t", lambda e, n, a=a: e.matmul(
                    out=ps_r[0:1, :], lhsT=ONESC2[:], rhs=PA[:, a * W:(a + 1) * W],
                    start=(a == 0), stop=(a == 15)), deps=bPA.rdeps() + bONESC2.rdeps() + (bpr.wdeps() if a == 0 else []))
                bPA.addr(tok)
            bpr.setw(tok)
            for a in range(16):
                tok = P.op("s", lambda e, n, a=a: e.activation(
                    out=PD[:, a * W:(a + 1) * W], in_=PA[:, a * W:(a + 1) * W], func=AF.Square),
                    deps=bPA.rdeps() + bPD.wdeps())
                bPD.setw(tok); bPA.addr(tok)
                tok2 = P.op("t", lambda e, n, a=a: e.matmul(
                    out=ps_r2[0:1, :], lhsT=ONESC2[:], rhs=PD[:, a * W:(a + 1) * W],
                    start=(a == 0), stop=(a == 15)), deps=bPD.rdeps() + bONESC2.rdeps() + (bpr2.wdeps() if a == 0 else []))
                bPD.addr(tok2)
            bpr2.setw(tok2)
            tm = P.op("v", lambda e, n: e.tensor_scalar(out=row(0), in0=ps_r[0:1, :], scalar1=1.0 / D,
                      scalar2=None, op0=ALU.mult), deps=bpr.rdeps() + bROWS.wdeps())
            bROWS.setw(tm)
            bpr.addr(tm)
            tsq = P.op("v", lambda e, n: e.tensor_scalar(out=row(3), in0=ps_r2[0:1, :], scalar1=1.0 / D,
                       scalar2=None, op0=ALU.mult), deps=bpr2.rdeps() + bROWS.wdeps())
            bROWS.setw(tsq)
            bpr2.addr(tsq)
            tm2 = P.op("s", lambda e, n: e.activation(out=row(4), in_=row(0), func=AF.Square), deps=[tm] + bROWS.wdeps())
            bROWS.setw(tm2)
            tv = P.op("v", lambda e, n: e.tensor_tensor(out=row(3), in0=row(3), in1=row(4), op=ALU.subtract), deps=[tsq, tm2] + bROWS.wdeps())
            bROWS.setw(tv)
            tsr = P.op("s", lambda e, n: e.activation(out=row(4), in_=row(3), func=AF.Sqrt, bias=CON[0:1, 0:1]),
                       deps=[tv] + bCON.rdeps() + bROWS.wdeps())
            bROWS.setw(tsr)
            trs = P.op("v", lambda e, n: e.reciprocal(out=row(1), in_=row(4)), deps=[tsr] + bROWS.wdeps())
            bROWS.setw(trs)
            tmr = P.op("v", lambda e, n: e.tensor_tensor(out=row(2), in0=row(0), in1=row(1), op=ALU.mult), deps=[trs, tm] + bROWS.wdeps())
            bROWS.setw(tmr)
            b_rstd = bcast(row(1), 0, [trs])
            b_mr = bcast(row(2), 1, [tmr])
            for a in range(16):
                asl = slice(a * W, (a + 1) * W)
                t1 = P.op("v", lambda e, n, asl=asl: e.tensor_tensor(out=PD[:, asl], in0=PA[:, asl], in1=bc(0), op=ALU.mult),
                          deps=bPA.rdeps() + [b_rstd] + bPD.wdeps())
                bPD.setw(t1); bPA.addr(t1); bBCP.addr(t1)
                t2 = P.op("v", lambda e, n, asl=asl: e.tensor_tensor(out=PD[:, asl], in0=PD[:, asl], in1=bc(1), op=ALU.subtract),
                          deps=[t1, b_mr] + bPD.wdeps())
                bPD.setw(t2); bBCP.addr(t2)
                t3 = P.op("v", lambda e, n, asl=asl: e.tensor_tensor(out=PC[:, asl], in0=PD[:, asl], in1=PB[:, asl], op=ALU.add),
                          deps=[t2] + bPB.rdeps() + bPC.wdeps())
                bPC.setw(t3); bPB.addr(t3); bPD.addr(t3)
            if debug:
                tokd = P.dma("g", lambda e, n, csl=csl: e.dma_start(
                    out=dbg_x2.rearrange("(a p) t -> p a t", p=128)[:, :, csl],
                    in_=PC[:].rearrange("p (a t) -> p a t", a=16)), deps=bPC.rdeps())
                bPC.addr(tokd); store_toks.append(tokd)
            trs2, tmr2 = stats(PC, bPC, 6, 0)
            b_rstd2 = bcast(row(7), 2, [trs2])
            b_mr2 = bcast(row(8), 3, [tmr2])
            for a in range(16):
                asl = slice(a * W, (a + 1) * W)
                t1 = P.op("v", lambda e, n, asl=asl: e.tensor_tensor(out=PD[:, asl], in0=PC[:, asl], in1=bc(2), op=ALU.mult),
                          deps=bPC.rdeps() + [b_rstd2] + bPD.wdeps())
                bPD.setw(t1); bPC.addr(t1); bBCP.addr(t1)
                t2 = P.op("v", lambda e, n, asl=asl: e.tensor_tensor(out=PD[:, asl], in0=PD[:, asl], in1=bc(3), op=ALU.subtract),
                          deps=[t1, b_mr2] + bPD.wdeps())
                bPD.setw(t2); bBCP.addr(t2)
            cm = SMALL[:, 0 * W:1 * W]
            cmn = SMALL[:, 12 * W:13 * W]
            tok = P.op("v", lambda e, n: e.tensor_copy(out=cm, in_=PD[:, 0:W]), deps=bPD.rdeps() + bSMALL.wdeps())
            bSMALL.setw(tok); bPD.addr(tok)
            tok = P.op("v", lambda e, n: e.tensor_copy(out=cmn, in_=PD[:, 0:W]), deps=bPD.rdeps() + bSMALL.wdeps())
            bSMALL.setw(tok); bPD.addr(tok)
            for a in range(1, 16):
                tok = P.op("v", lambda e, n, a=a: e.tensor_tensor(out=cm, in0=PD[:, a * W:(a + 1) * W], in1=cm,
                           op=ALU.max), deps=bPD.rdeps() + bSMALL.rdeps() + bSMALL.wdeps())
                bSMALL.setw(tok); bPD.addr(tok)
                tok = P.op("v", lambda e, n, a=a: e.tensor_tensor(out=cmn, in0=PD[:, a * W:(a + 1) * W], in1=cmn,
                           op=ALU.min), deps=bPD.rdeps() + bSMALL.rdeps() + bSMALL.wdeps())
                bSMALL.setw(tok); bPD.addr(tok)
            tok = P.op("v", lambda e, n: e.tensor_scalar(out=cmn, in0=cmn, scalar1=-1.0, scalar2=None, op0=ALU.mult),
                       deps=bSMALL.rdeps() + bSMALL.wdeps())
            bSMALL.setw(tok)
            tok = P.op("v", lambda e, n: e.tensor_tensor(out=cm, in0=cmn, in1=cm, op=ALU.max),
                       deps=bSMALL.rdeps() + bSMALL.wdeps())
            bSMALL.setw(tok)
            tok = P.op("g", lambda e, n: e.partition_all_reduce(out_ap=SMALL[:, 1 * W:2 * W], in_ap=cm,
                       channels=128, reduce_op=ReduceOp.max), deps=bSMALL.rdeps() + bSMALL.wdeps())
            bSMALL.setw(tok)
            am = SMALL[:, 1 * W:2 * W]
            sxb = SMALL[:, 2 * W:3 * W]
            sxib = SMALL[:, 3 * W:4 * W]
            t1 = P.op("v", lambda e, n: e.tensor_scalar(out=am, in0=am, scalar1=1e-5, scalar2=None, op0=ALU.max),
                      deps=bSMALL.rdeps() + bSMALL.wdeps())
            bSMALL.setw(t1)
            t2 = P.op("v", lambda e, n: e.reciprocal(out=sxb, in_=am), deps=[t1] + bSMALL.wdeps())
            bSMALL.setw(t2)
            t3 = P.op("v", lambda e, n: e.tensor_scalar(out=sxb, in0=sxb, scalar1=127.0, scalar2=None, op0=ALU.mult),
                      deps=[t2] + bSMALL.wdeps())
            bSMALL.setw(t3)
            t4 = P.op("v", lambda e, n: e.tensor_scalar(out=sxib, in0=am, scalar1=1.0 / 127.0, scalar2=None, op0=ALU.mult),
                      deps=[t1] + bSMALL.wdeps())
            bSMALL.setw(t4)
            for a in range(16):
                asl = slice(a * W, (a + 1) * W)
                t5 = P.op("v", lambda e, n, asl=asl: e.tensor_tensor(out=PD[:, asl], in0=PD[:, asl], in1=sxb, op=ALU.mult),
                          deps=bPD.rdeps() + [t3] + bPD.wdeps())
                bPD.setw(t5); bSMALL.addr(t5)
                t6 = P.op("v", lambda e, n, asl=asl: e.tensor_scalar(out=QF[:, asl], in0=PD[:, asl], scalar1=12584448.0,
                          scalar2=-12582912.0, op0=ALU.add, op1=ALU.add), deps=[t5] + bQF.wdeps())
                bQF.setw(t6); bPD.addr(t6)
                t7 = P.op("v", lambda e, n, asl=asl: e.tensor_scalar(out=XQ[:, asl], in0=QF[:, asl], scalar1=-1536.0,
                          scalar2=None, op0=ALU.add), deps=[t6] + bXQ.wdeps())
                bXQ.setw(t7); bQF.addr(t7)

            if debug:
                for a in range(16):
                    asl = slice(a * W, (a + 1) * W)
                    td = P.op("v", lambda e, n, asl=asl: e.tensor_tensor(out=PD[:, asl], in0=XQ[:, asl], in1=sxib, op=ALU.mult),
                              deps=bXQ.rdeps() + [t4] + bPD.wdeps())
                    bPD.setw(td); bXQ.addr(td); bSMALL.addr(td)
                tokd = P.dma("g", lambda e, n, csl=csl: e.dma_start(
                    out=dbg_xq.rearrange("(a p) t -> p a t", p=128)[:, :, csl],
                    in_=PD[:].rearrange("p (a t) -> p a t", a=16)), deps=bPD.rdeps())
                bPD.addr(tokd); store_toks.append(tokd)
            w1s = lambda buf: UB[:, buf * 2048:(buf + 1) * 2048]
            w3s = lambda buf: UB[:, (2 + buf) * 2048:(3 + buf) * 2048]
            hm = SMALL[:, 4 * W:5 * W]
            t1t = SMALL[:, 5 * W:6 * W]
            t3t = SMALL[:, 6 * W:7 * W]
            hft = SMALL[:, 7 * W:8 * W]
            hmn = SMALL[:, 13 * W:14 * W]
            bHM, bT1, bT3, bHFT = B(), B(), B(), B()
            hf_wtoks_old = hf_wtoks if hp > 0 else []
            hf_rtoks_old = hf_rtoks if hp > 0 else []
            hf_wtoks, hf_rtoks = [], hf_rtoks_old
            for ht in range(HT):
                buf = ht % 2
                tw1 = P.dma("y", lambda e, n, ht=ht, buf=buf: e.dma_start(
                    out=w1s(buf).rearrange("p (a c) -> p a c", a=16),
                    in_=w1T.rearrange("(a p) h -> p a h", p=128)[:, :, ht * 128:(ht + 1) * 128]),
                    deps=bUB.wdeps())
                bUB.setw(tw1)
                tw3 = P.dma("y", lambda e, n, ht=ht, buf=buf: e.dma_start(
                    out=w3s(buf).rearrange("p (a c) -> p a c", a=16),
                    in_=w3T.rearrange("(a p) h -> p a h", p=128)[:, :, ht * 128:(ht + 1) * 128]),
                    deps=bUB.wdeps())
                bUB.setw(tw3)
                def wq(src_fn, bSrc, m, qsl, deps_extra):
                    tq0 = P.op("v", lambda e, n, src_fn=src_fn, m=m: e.tensor_scalar(
                        out=src_fn, in0=src_fn, scalar1=swinv_c(m), scalar2=None, op0=ALU.mult),
                        deps=bSrc.rdeps() + bSWB.rdeps() + bSrc.wdeps())
                    bSrc.setw(tq0)
                    tq1 = P.op("v", lambda e, n, src_fn=src_fn: e.tensor_scalar(
                        out=src_fn, in0=src_fn, scalar1=-1.45, scalar2=1.45, op0=ALU.max, op1=ALU.min),
                        deps=bSrc.rdeps() + bSrc.wdeps())
                    bSrc.setw(tq1)
                    tq2 = P.op("v", lambda e, n, src_fn=src_fn: e.tensor_scalar(
                        out=MGF[:], in0=src_fn, scalar1=12584448.0, scalar2=-12582912.0,
                        op0=ALU.add, op1=ALU.add),
                        deps=bSrc.rdeps() + bMGF.wdeps())
                    bMGF.setw(tq2); bSrc.addr(tq2)
                    tq3 = P.op("v", lambda e, n, qsl=qsl: e.tensor_scalar(
                        out=WQT[:, qsl], in0=MGF[:], scalar1=-1536.0, scalar2=None, op0=ALU.add),
                        deps=bMGF.rdeps() + bWQT.wdeps())
                    bWQT.setw(tq3); bMGF.addr(tq3)
                    return tq3
                tq1 = wq(w1s(buf), bUB, 0, slice(0, 2048), [])
                tq3 = wq(w3s(buf), bUB, 1, slice(2048, 4096), [])
                for a in range(16):
                    tokm = P.op("t", lambda e, n, a=a: e.matmul(
                        out=ps_a[:, :], lhsT=WQT[:, a * 128:(a + 1) * 128], rhs=XQ[:, a * W:(a + 1) * W],
                        start=(a == 0), stop=(a == 15)),
                        deps=[tq1] + bXQ.rdeps() + (bpa.wdeps() if a == 0 else []))
                    bXQ.addr(tokm); bWQT.addr(tokm)
                bpa.setw(tokm)
                for a in range(16):
                    tokm = P.op("t", lambda e, n, a=a: e.matmul(
                        out=ps_b[:, :], lhsT=WQT[:, 2048 + a * 128:2048 + (a + 1) * 128], rhs=XQ[:, a * W:(a + 1) * W],
                        start=(a == 0), stop=(a == 15)),
                        deps=[tq3] + bXQ.rdeps() + (bpb.wdeps() if a == 0 else []))
                    bXQ.addr(tokm); bWQT.addr(tokm)
                bpb.setw(tokm)
                tt1 = P.op("v", lambda e, n: e.tensor_tensor(out=t1t, in0=ps_a[:, :], in1=sxib, op=ALU.mult),
                           deps=bpa.rdeps() + [t4] + bT1.wdeps())
                bT1.setw(tt1); bpa.addr(tt1); bSMALL.addr(tt1)
                ts1 = P.op("s", lambda e, n: e.activation(out=t1t, in_=t1t, func=AF.Silu, scale=swm_c(0)),
                           deps=[tt1] + bSWB.rdeps() + bT1.wdeps())
                bT1.setw(ts1)
                tt3 = P.op("v", lambda e, n: e.tensor_tensor(out=t3t, in0=ps_b[:, :], in1=sxib, op=ALU.mult),
                           deps=bpb.rdeps() + [t4] + bT3.wdeps())
                bT3.setw(tt3); bpb.addr(tt3); bSMALL.addr(tt3)
                thf = P.op("v", lambda e, n: e.scalar_tensor_tensor(out=hft, in0=t1t, scalar=swm_c(1), in1=t3t,
                           op0=ALU.mult, op1=ALU.mult), deps=[ts1, tt3] + bSWB.rdeps() + bHFT.wdeps())
                bHFT.setw(thf); bT1.addr(thf); bT3.addr(thf)
                if ht == 0:
                    thm = P.op("v", lambda e, n: e.tensor_copy(out=hm, in_=hft), deps=[thf] + bHM.wdeps())
                    bHM.setw(thm); bHFT.addr(thm)
                    thm = P.op("v", lambda e, n: e.tensor_copy(out=hmn, in_=hft), deps=[thf] + bHM.wdeps())
                    bHM.setw(thm); bHFT.addr(thm)
                else:
                    thm = P.op("v", lambda e, n: e.tensor_tensor(out=hm, in0=hft, in1=hm, op=ALU.max),
                               deps=[thf] + bHM.rdeps() + bHM.wdeps())
                    bHM.setw(thm); bHFT.addr(thm)
                    thm = P.op("v", lambda e, n: e.tensor_tensor(out=hmn, in0=hft, in1=hmn, op=ALU.min),
                               deps=[thf] + bHM.rdeps() + bHM.wdeps())
                    bHM.setw(thm); bHFT.addr(thm)
                tsp = P.dma("g", lambda e, n, ht=ht: e.dma_start(out=hfs[ht * 128:(ht + 1) * 128, :], in_=hft),
                            deps=[thf] + (hf_rtoks if ht == 0 else []))
                hf_wtoks.append(tsp)
                bHFT.addr(tsp)
                if debug:
                    tspd = P.dma("g", lambda e, n, ht=ht, csl=csl: e.dma_start(
                        out=dbg_hf[ht * 128:(ht + 1) * 128, csl], in_=hft), deps=[thf])
                    bHFT.addr(tspd); store_toks.append(tspd)
            tok = P.op("v", lambda e, n: e.tensor_scalar(out=hmn, in0=hmn, scalar1=-1.0, scalar2=None, op0=ALU.mult),
                       deps=bHM.rdeps() + bHM.wdeps())
            bHM.setw(tok)
            tok = P.op("v", lambda e, n: e.tensor_tensor(out=hm, in0=hmn, in1=hm, op=ALU.max),
                       deps=bHM.rdeps() + bHM.wdeps())
            bHM.setw(tok)
            tok = P.op("g", lambda e, n: e.partition_all_reduce(out_ap=SMALL[:, 8 * W:9 * W], in_ap=hm,
                       channels=128, reduce_op=ReduceOp.max), deps=bHM.rdeps() + bSMALL.wdeps())
            bSMALL.setw(tok); bHM.addr(tok)
            amh = SMALL[:, 8 * W:9 * W]
            shb = SMALL[:, 9 * W:10 * W]
            shib = SMALL[:, 10 * W:11 * W]
            t1h = P.op("v", lambda e, n: e.tensor_scalar(out=amh, in0=amh, scalar1=1e-5, scalar2=None, op0=ALU.max),
                       deps=bSMALL.rdeps() + bSMALL.wdeps())
            bSMALL.setw(t1h)
            t2h = P.op("v", lambda e, n: e.reciprocal(out=shb, in_=amh), deps=[t1h] + bSMALL.wdeps())
            bSMALL.setw(t2h)
            t3h = P.op("v", lambda e, n: e.tensor_scalar(out=shb, in0=shb, scalar1=127.0, scalar2=None, op0=ALU.mult),
                       deps=[t2h] + bSMALL.wdeps())
            bSMALL.setw(t3h)
            t4h = P.op("v", lambda e, n: e.tensor_scalar(out=shib, in0=amh, scalar1=1.0 / 127.0, scalar2=None, op0=ALU.mult),
                       deps=[t1h] + bSMALL.wdeps())
            bSMALL.setw(t4h)
            QH = UB[:].bitcast(BF16)
            for ht in range(HT):
                trd = P.dma("y", lambda e, n, ht=ht: e.dma_start(out=hft, in_=hfs[ht * 128:(ht + 1) * 128, :]),
                            deps=(hf_wtoks if ht == 0 else []) + bHFT.wdeps())
                bHFT.setw(trd); hf_rtoks.append(trd)
                tmq = P.op("v", lambda e, n: e.tensor_tensor(out=hft, in0=hft, in1=shb, op=ALU.mult),
                           deps=[trd, t3h] + bHFT.wdeps())
                bHFT.setw(tmq); bSMALL.addr(tmq)
                tf = P.op("v", lambda e, n: e.tensor_scalar(out=QF[:, 0:W], in0=hft, scalar1=12584448.0, scalar2=-12582912.0,
                          op0=ALU.add, op1=ALU.add), deps=[tmq] + bQF.wdeps())
                bQF.setw(tf); bHFT.addr(tf)
                tb = P.op("v", lambda e, n, ht=ht: e.tensor_scalar(out=QH[:, ht * W:(ht + 1) * W], in0=QF[:, 0:W],
                          scalar1=-1536.0, scalar2=None, op0=ALU.add), deps=[tf] + bUB.wdeps())
                bUB.setw(tb); bQF.addr(tb)
            for dt_ in range(16):
                tw2 = P.dma("y", lambda e, n, dt_=dt_: e.dma_start(
                    out=W2B[:].rearrange("p (a c) -> p a c", a=44),
                    in_=w2T.rearrange("(a p) d -> p a d", p=128)[:, :, dt_ * 128:(dt_ + 1) * 128]),
                    deps=bW2B.wdeps())
                bW2B.setw(tw2)
                tq0 = P.op("v", lambda e, n: e.tensor_scalar(out=W2B[:], in0=W2B[:], scalar1=swinv_c(2), scalar2=None,
                           op0=ALU.mult), deps=bW2B.rdeps() + bSWB.rdeps() + bW2B.wdeps())
                bW2B.setw(tq0)
                tq1 = P.op("v", lambda e, n: e.tensor_scalar(out=W2B[:], in0=W2B[:], scalar1=-1.45, scalar2=1.45,
                           op0=ALU.max, op1=ALU.min), deps=bW2B.rdeps() + bW2B.wdeps())
                bW2B.setw(tq1)
                tq2a = P.op("v", lambda e, n: e.tensor_scalar(out=MGF[:], in0=W2B[:, 0:2048], scalar1=12584448.0,
                            scalar2=-12582912.0, op0=ALU.add, op1=ALU.add), deps=bW2B.rdeps() + bSWB.rdeps() + bMGF.wdeps())
                bMGF.setw(tq2a)
                tq3a = P.op("v", lambda e, n: e.tensor_scalar(out=W2Q[:, 0:2048], in0=MGF[:], scalar1=-1536.0,
                            scalar2=None, op0=ALU.add), deps=[tq2a] + bW2Q.wdeps())
                bW2Q.setw(tq3a); bMGF.addr(tq3a)
                tq2b = P.op("v", lambda e, n: e.tensor_scalar(out=MGF[:], in0=W2B[:, 2048:4096], scalar1=12584448.0,
                            scalar2=-12582912.0, op0=ALU.add, op1=ALU.add), deps=bW2B.rdeps() + bSWB.rdeps() + bMGF.wdeps() + [tq3a])
                bMGF.setw(tq2b); bW2B.addr(tq2b)
                tq3b = P.op("v", lambda e, n: e.tensor_scalar(out=W2Q[:, 2048:4096], in0=MGF[:], scalar1=-1536.0,
                            scalar2=None, op0=ALU.add), deps=[tq2b] + bW2Q.wdeps())
                bW2Q.setw(tq3b); bMGF.addr(tq3b)
                tq2c = P.op("v", lambda e, n: e.tensor_scalar(out=MGF[:, 0:1536], in0=W2B[:, 4096:5632], scalar1=12584448.0,
                            scalar2=-12582912.0, op0=ALU.add, op1=ALU.add), deps=bW2B.rdeps() + bSWB.rdeps() + bMGF.wdeps() + [tq3b])
                bMGF.setw(tq2c); bW2B.addr(tq2c)
                tq3c = P.op("v", lambda e, n: e.tensor_scalar(out=W2Q[:, 4096:5632], in0=MGF[:, 0:1536], scalar1=-1536.0,
                            scalar2=None, op0=ALU.add), deps=[tq2c] + bW2Q.wdeps())
                bW2Q.setw(tq3c); bMGF.addr(tq3c)
                for ht in range(HT):
                    tokm = P.op("t", lambda e, n, ht=ht: e.matmul(
                        out=ps_c[:, :], lhsT=W2Q[:, ht * 128:(ht + 1) * 128], rhs=QH[:, ht * W:(ht + 1) * W],
                        start=(ht == 0), stop=(ht == HT - 1)),
                        deps=[tq3c] + bUB.rdeps() + (bpc.wdeps() if ht == 0 else []))
                    bUB.addr(tokm); bW2Q.addr(tokm)
                bpc.setw(tokm)
                te1 = P.op("s", lambda e, n, dt_=dt_: e.activation(out=PD[:, dt_ * W:(dt_ + 1) * W], in_=ps_c[:, :],
                           func=AF.Copy, scale=swm_c(2)), deps=bpc.rdeps() + bSWB.rdeps() + bPD.wdeps())
                bPD.setw(te1); bpc.addr(te1)
                te2 = P.op("v", lambda e, n, dt_=dt_: e.tensor_tensor(out=PD[:, dt_ * W:(dt_ + 1) * W],
                           in0=PD[:, dt_ * W:(dt_ + 1) * W], in1=shib, op=ALU.mult), deps=[te1, t4h] + bPD.wdeps())
                bPD.setw(te2); bSMALL.addr(te2)
            if debug:
                tokd = P.dma("g", lambda e, n, csl=csl: e.dma_start(
                    out=dbg_b2.rearrange("(a p) t -> p a t", p=128)[:, :, csl],
                    in_=PD[:].rearrange("p (a t) -> p a t", a=16)), deps=bPD.rdeps())
                bPD.addr(tokd); store_toks.append(tokd)
            trs3, tmr3 = stats(PD, bPD, 10, 0)
            b_rstd3 = bcast(row(11), 4, [trs3])
            b_mr3 = bcast(row(12), 5, [tmr3])
            for a in range(16):
                asl = slice(a * W, (a + 1) * W)
                t1o = P.op("v", lambda e, n, asl=asl: e.tensor_tensor(out=PB[:, asl], in0=PD[:, asl], in1=bc(4), op=ALU.mult),
                           deps=bPD.rdeps() + [b_rstd3] + bPB.wdeps())
                bPB.setw(t1o); bPD.addr(t1o); bBCP.addr(t1o)
                t2o = P.op("v", lambda e, n, asl=asl: e.tensor_tensor(out=PB[:, asl], in0=PB[:, asl], in1=bc(5), op=ALU.subtract),
                           deps=[t1o, b_mr3] + bPB.wdeps())
                bPB.setw(t2o); bBCP.addr(t2o)
                t3o = P.op("v", lambda e, n, asl=asl: e.tensor_tensor(out=PB[:, asl], in0=PB[:, asl], in1=PC[:, asl], op=ALU.add),
                           deps=[t2o] + bPC.rdeps() + bPB.wdeps())
                bPB.setw(t3o); bPC.addr(t3o)
            tko = P.dma("g", lambda e, n, csl=csl: e.dma_start(
                out=out_o.rearrange("(a p) t -> p a t", p=128)[:, :, csl],
                in_=PB[:].rearrange("p (a t) -> p a t", a=16)), deps=bPB.rdeps())
            bPB.addr(tko); store_toks.append(tko)

        P.barrier("g", store_toks)
        block = E(nc.Block())
        P.build_blocks(block)
    return finalize(nc)


L_FULL = 8192
LC = L_FULL // 8

def kernel(L0_attn_x, L0_attn_W_xproj, L0_attn_W_dt, L0_attn_b_dt, L0_attn_A_log,
           L0_attn_D_param, L0_ffn_w1, L0_ffn_w2, L0_ffn_w3):
    from concourse.bass_utils import run_bass_kernel_spmd
    x = np.ascontiguousarray(np.asarray(L0_attn_x, np.float32)[0])
    W_xproj = np.asarray(L0_attn_W_xproj, np.float32)
    W_dt = np.asarray(L0_attn_W_dt, np.float32)
    b_dt = np.asarray(L0_attn_b_dt, np.float32)
    A_log = np.asarray(L0_attn_A_log, np.float32)
    D_param = np.asarray(L0_attn_D_param, np.float32)
    w1 = np.asarray(L0_ffn_w1, np.float32)
    w2 = np.asarray(L0_ffn_w2, np.float32)
    w3 = np.asarray(L0_ffn_w3, np.float32)

    xT = np.ascontiguousarray(x.T)
    wxpT = np.ascontiguousarray(W_xproj.T)
    ones1 = np.ones((1, 128), np.float32)
    onesc = np.ones((128, 1), np.float32)
    consts = np.zeros((128, 8), np.float32)
    consts[:, 0] = 1e-5; consts[:, 1] = 1.0; consts[:, 2] = 1536.0; consts[:, 3] = -1536.0
    ident = np.eye(128, dtype=np.float32)
    sel32 = np.zeros((32, 32 * 128), np.float32)
    for j in range(32):
        sel32[j, j * 128:(j + 1) * 128] = 1.0

    nc1 = build_L1(LC)
    in1 = [{"xT": np.ascontiguousarray(xT[:, i * LC:(i + 1) * LC]), "wxpT": wxpT,
            "ones1": ones1, "onesc": onesc, "consts": consts} for i in range(8)]
    r1 = run_bass_kernel_spmd(nc1, in1, core_ids=list(range(8)))
    xn1T = np.concatenate([r_["xn1T"] for r_ in r1.results], axis=1)
    xdblT = np.concatenate([r_["xdblT"] for r_ in r1.results], axis=1)

    WdtT = np.ascontiguousarray(W_dt.T)
    wcols = w1.size // 8 // 128
    nc2 = build_L2(L_FULL, wcols)
    in2 = []
    for i in range(8):
        sl = slice(256 * i, 256 * (i + 1))
        al = A_log[sl].reshape(2, 128, 16)
        al = np.ascontiguousarray(al.transpose(1, 2, 0).reshape(128, 32))
        in2.append({
            "xn1T": np.ascontiguousarray(xn1T[sl]),
            "xdblT": xdblT,
            "wdtT": np.ascontiguousarray(WdtT[:, sl]),
            "bdt": np.ascontiguousarray(b_dt[sl].reshape(2, 128).T),
            "alog": al,
            "dpar": np.ascontiguousarray(D_param[sl].reshape(2, 128).T),
            "ones1": ones1, "ident": ident, "consts": consts, "sel32": sel32,
            "w1a": np.ascontiguousarray(w1.reshape(8, 128, -1)[i]),
            "w3a": np.ascontiguousarray(w3.reshape(8, 128, -1)[i]),
            "w2a": np.ascontiguousarray(w2.reshape(8, 128, -1)[i]),
        })
    r2 = run_bass_kernel_spmd(nc2, in2, core_ids=list(range(8)))
    yT = np.concatenate([r_["yT"] for r_ in r2.results], axis=0)
    wsums = np.stack([r_["wsum"] for r_ in r2.results])

    wq = np.zeros((128, 48), np.float32)
    for m in range(3):
        part = wsums[:, :, 2 * m:2 * m + 2]
        wq[:, m * 16:(m + 1) * 16] = np.ascontiguousarray(part.transpose(1, 0, 2)).reshape(128, 16)
    w1T = np.ascontiguousarray(w1.T)
    w3T = np.ascontiguousarray(w3.T)
    w2T = np.ascontiguousarray(w2.T)
    nc3 = build_L3(LC)
    in3 = []
    for i in range(8):
        sl = slice(LC * i, LC * (i + 1))
        in3.append({"xT": np.ascontiguousarray(xT[:, sl]),
                    "yT": np.ascontiguousarray(yT[:, sl]),
                    "w1T": w1T, "w3T": w3T, "w2T": w2T, "wqs": wq,
                    "onesf": ones1, "onescr": onesc, "onescf": onesc, "consts": consts})
    r3 = run_bass_kernel_spmd(nc3, in3, core_ids=list(range(8)))
    outT = np.concatenate([r_["outT"] for r_ in r3.results], axis=1)
    return np.ascontiguousarray(outT.T)[None].astype(np.float32)
